# version 04
# baseline (speedup 1.0000x reference)
import math
import numpy as np
import contextlib

import concourse.bass as bass
import concourse.mybir as mybir
from concourse.bass_utils import run_bass_kernel_spmd

F32 = mybir.dt.float32
BF16 = mybir.dt.bfloat16
AF = mybir.ActivationFunctionType
ALU = mybir.AluOpType
AX = mybir.AxisListType

S = 2048
DM = 2048
NCH = 16
NTT = 16
HA, DA, KVA = 16, 128, 4
HB, DB, KVB = 32, 64, 8
FH = 5632
NHT = 44
EPS_LN = 1e-5
EPS_RMS = 1e-6
EPS_Q = 1e-8
SCALE_A = 1.0 / math.sqrt(DA)
WAVE_B = 150000.0
NSINK = 640.0

DBG_PHASE = None


class Ctr:
    def __init__(self, h, name):
        self.h = h
        self.name = name
        self.n = 0

    def inc(self, ins, k=1):
        ins.then_inc(self.h, k)
        self.n += k
        return self.n


class C:
    pass


def build(c):
    nc = c.nc
    st = c.st

    def din(name, shape, dt=F32):
        return nc.dram_tensor(name, shape, dt, kind="ExternalInput").ap()

    c.xT = din("xT", [DM, S])
    c.cosA = din("cosA", [128, S])
    c.sinA = din("sinA", [128, S])
    c.qnw = din("qnw", [128, 1])
    c.knw = din("knw", [128, 1])
    c.wq = din("wq", [DM, HA * DA])
    c.wk = din("wk", [DM, KVA * DA])
    c.wv = din("wv", [DM, KVA * DA])
    c.wo = din("wo", [HA * DA, DM])
    c.wqB = din("wqB", [DM, HB * DB])
    c.wkB = din("wkB", [DM, KVB * DB])
    c.wvB = din("wvB", [DM, KVB * DB])
    c.woB = din("woB", [HB * DB, DM])
    c.bq16 = din("bq16", [128, 16])
    c.bk4 = din("bk4", [128, 4])
    c.bvr = din("bvr", [1, KVB * DB])
    c.bo16 = din("bo16", [128, 16])
    c.sinks = din("sinks", [1, HB])
    c.cosB = din("cosB", [128, S])
    c.sinBs = din("sinBs", [128, S])
    c.mskL = din("mskL", [128, 128])
    c.mskR = din("mskR", [128, 128])
    c.w1T = din("w1T", [DM, FH])
    c.w2T = din("w2T", [DM, FH])
    c.w3T = din("w3T", [FH, DM])

    c.yT = nc.dram_tensor("yT", [DM, S], F32, kind="ExternalOutput").ap()

    c.xm = nc.dram_tensor("xm", [DM, S], F32, kind="Internal").ap()
    c.w1q = nc.dram_tensor("w1q", [NHT, 128, NCH, 128], BF16, kind="Internal").ap()
    c.w2q = nc.dram_tensor("w2q", [NHT, 128, NCH, 128], BF16, kind="Internal").ap()
    c.w3q = nc.dram_tensor("w3q", [16, 128, NHT, 128], BF16, kind="Internal").ap()
    c.ktdA = nc.dram_tensor("ktdA", [KVA, 128, S], BF16, kind="Internal").ap()
    c.vtdA = nc.dram_tensor("vtdA", [NTT, 128, KVA * 128], BF16, kind="Internal").ap()
    c.ktdB = nc.dram_tensor("ktdB", [KVB, 128, S], BF16, kind="Internal").ap()
    c.vtdB = nc.dram_tensor("vtdB", [NTT, 128, KVB * 64], BF16, kind="Internal").ap()
    c.dnx = nc.dram_tensor("dnx", [32, S], F32, kind="Internal").ap()
    c.dnx2a = nc.dram_tensor("dnx2a", [128, 256], F32, kind="Internal").ap()
    c.qscr = nc.dram_tensor("qscr", [1, 128], F32, kind="Internal").ap()
    c.dnx2b = nc.dram_tensor("dnx2b", [128, 512], F32, kind="Internal").ap()

    c.dbg = {}
    if DBG_PHASE is not None:
        for nm, shape, dt in [("d_a1", [128, 32768], BF16), ("d_a2", [128, 32768], BF16),
                              ("d_sh", [128, 12288], BF16), ("d_acc", [128, 2048], F32),
                              ("d_cr", [1, 320], F32), ("d_ps", [128, 2048], F32)]:
            c.dbg[nm] = nc.dram_tensor(nm, shape, dt, kind="ExternalOutput").ap()

    def sb(name, shape, dt):
        return st.enter_context(nc.sbuf_tensor(name, shape, dt))

    c.A1 = sb("A1", [128, 32768], BF16)
    c.A2 = sb("A2", [128, 32768], BF16)
    c.H2 = sb("H2", [128, 12288], BF16)
    c.SH = sb("SH", [128, 12288], BF16)
    c.PT0 = sb("PT0", [128, 2048], BF16)
    c.PT1 = sb("PT1", [128, 2048], BF16)
    c.ACC = sb("ACC", [128, 2048], F32)
    c.XA = sb("XA", [128, 1024], F32)
    c.XB = sb("XB", [128, 1024], F32)
    c.ML = sb("ML", [128, 128], BF16)
    c.MR = sb("MR", [128, 128], BF16)
    c.OC = sb("OC", [128, 1], BF16)
    c.OCF = sb("OCF", [128, 1], F32)
    c.ORW = sb("ORW", [1, 128], BF16)
    c.WRQ = sb("WRQ", [1, 128], BF16)
    c.WRK = sb("WRK", [1, 128], BF16)
    c.NEGC = sb("NEGC", [128, 1], F32)
    c.SNK = sb("SNK", [128, 33], F32)
    c.CR = sb("CR", [1, 320], F32)
    c.BQS = sb("BQS", [128, 36], F32)
    c.CST = sb("CST", [128, 4], F32)
    c.ZRW = sb("ZRW", [1, 128], BF16)
    c.GAC = sb("GAC", [128, 1], F32)
    c.GTMP = sb("GTMP", [128, 1], F32)
    c.GACM = sb("GACM", [128, 96], F32)
    c.IGB3 = sb("IGB3", [128, 4], F32)
    c.IGB = sb("IGB", [128, 1], F32)
    c.SNKR = sb("SNKR", [1, 33], F32)
    c.RB1 = sb("RB1", [1, 2048], BF16)
    c.WS = c.SH[:, 0:4096]
    c.SQ = c.H2[:, 8192:10240]
    c.SW = c.H2[:, 10240:12288]
    c.KST = c.PT1[:, 1024:1280]
    c.VST = c.PT1[:, 1280:1536]
    c.NRQ = c.CR[0:1, 64:192]
    c.NRK = c.CR[0:1, 192:320]
    c.BVR = c.RB1[0:1, 1024:1536]

    c.TCOS = c.SH[:, 8192:10240]
    c.TSIN = c.SH[:, 10240:12288]
    c.H2f = c.H2[:].bitcast(F32)
    c.A1f = c.A1[:].bitcast(F32)
    c.A2f = c.A2[:].bitcast(F32)

    c.PSA = st.enter_context(nc.psum_tensor("PSA", [128, 2048], F32))
    c.PSB = st.enter_context(nc.psum_tensor("PSB", [128, 2048], F32))

    c.sems = {}
    for nm in ["tt", "vv", "aa",
               "g0", "g1", "gs", "gr",
               "s0", "s1", "skd",
               "x0", "x1", "d0", "d1",
               "qv", "qa", "qg0", "qg1", "qw0", "qw1"]:
        c.sems[nm] = Ctr(st.enter_context(nc.semaphore(nm)), nm)
    c.TT = c.sems["tt"]
    c.VV = c.sems["vv"]
    c.AA = c.sems["aa"]
    c.all_ctrs = list(c.sems.values())


def snap(c):
    return {k.name: k.n for k in c.all_ctrs}


def wsnap(c, eng, sn):
    for k in c.all_ctrs:
        if sn[k.name] > 0:
            eng.wait_ge(k.h, sn[k.name])


def dbg_dump(c, tag):
    if DBG_PHASE != tag:
        return False
    b = c.block
    sn = snap(c)

    SS0 = c.sems["s0"]

    @b.sync
    def _(s):
        wsnap(c, s, sn)
        s.dma_start(out=c.dbg["d_a1"], in_=c.A1[:]).then_inc(SS0.h, 16)
        s.dma_start(out=c.dbg["d_a2"], in_=c.A2[:]).then_inc(SS0.h, 16)
        s.dma_start(out=c.dbg["d_sh"], in_=c.SH[:]).then_inc(SS0.h, 16)
        s.dma_start(out=c.dbg["d_acc"], in_=c.ACC[:]).then_inc(SS0.h, 16)
        ins = s.dma_start(out=c.dbg["d_cr"], in_=c.CR[:])
        SS0.n += 64
        SS0.inc(ins, 16)
        s.wait_ge(SS0.h, SS0.n)
    return True


def ph_init(c):
    b = c.block

    @b.vector
    def _(v):
        for big in (c.A1, c.A2, c.H2, c.SH, c.PT0, c.PT1, c.ACC, c.XA, c.XB,
                    c.RB1, c.CR, c.SNK, c.NEGC, c.IGB, c.GAC, c.GTMP, c.SNKR,
                    c.BQS, c.WRQ, c.WRK, c.ML, c.MR):
            v.memset(big[:], 0.0)
        v.memset(c.OC[:], 1.0)
        v.memset(c.OCF[:], 1.0)
        v.memset(c.CST[:, 0:1], -0.5)
        v.memset(c.CST[:, 1:2], 0.5)
        v.memset(c.CST[:, 2:3], EPS_LN)
        v.memset(c.CST[:, 3:4], EPS_RMS)
        v.memset(c.ZRW[:], 0.0)
        ins = v.memset(c.ORW[:], 1.0)
        c.VV.inc(ins)

    sninit = snap(c)

    @b.gpsimd
    def _(g):
        g.wait_ge(c.VV.h, sninit["vv"])
        for out, src in [
            (c.ML[:], c.mskL), (c.MR[:], c.mskR),
            (c.SNKR[0:1, 0:32], c.sinks),
            (c.BQS[:, 0:16], c.bq16), (c.BQS[:, 16:20], c.bk4),
            (c.BQS[:, 20:36], c.bo16),
            (c.NRQ[:], c.qnw.rearrange("p o -> o p")),
            (c.NRK[:], c.knw.rearrange("p o -> o p")),
            (c.WRQ[:], c.qnw.rearrange("p o -> o p")),
            (c.WRK[:], c.knw.rearrange("p o -> o p")),
        ]:
            ins = g.dma_start(out=out, in_=src)
            c.sems["g0"].inc(ins, 16)


def ph_quant(c, src, nrow, ncol, wrchunk):
    b = c.block
    nchk = nrow // 128
    wbuf = [c.A1f[:, 0:ncol], c.A1f[:, 8192:8192 + ncol]]
    rbuf = [c.A2f[:, 0:ncol], c.A2f[:, 8448:8448 + ncol]]
    s1 = c.SH[:, 0:ncol]
    s2 = c.SH[:, 6144:6144 + ncol]
    qb = [c.H2[:, 0:ncol], c.A2[:, 11264:11264 + ncol]]

    sn0 = snap(c)
    ld1, red1 = {}, {}
    SSP = [c.sems["s0"], c.sems["s1"]]

    @b.sync
    def _(s):
        wsnap(c, s, sn0)
        for i in range(nchk):
            if i >= 2:
                s.wait_ge(c.VV.h, sn0["vv"] + (i - 1))
            ins = s.dma_start(out=wbuf[i % 2], in_=src[i * 128:(i + 1) * 128, :])
            ld1[i] = SSP[i % 2].inc(ins, 16)

    @b.vector
    def _(v):
        for i in range(nchk):
            v.wait_ge(SSP[i % 2].h, ld1[i])
            ins = v.tensor_reduce(c.GACM[:, i:i + 1], wbuf[i % 2], axis=AX.X,
                                  op=ALU.add, apply_absolute_value=True)
            red1[i] = c.VV.inc(ins)
        v.wait_ge(c.VV.h, red1[nchk - 1])
        ins = v.tensor_reduce(c.GAC[:], c.GACM[:, 0:nchk], axis=AX.X, op=ALU.add)
        c.VV.inc(ins)

    sn1 = snap(c)

    @b.tensor
    def _(t):
        t.wait_ge(c.VV.h, sn1["vv"])
        ins = t.matmul(c.PSA[0:1, 0:1], c.OCF[:], c.GAC[:], start=True, stop=True)
        c.TT.inc(ins)

    sn2 = snap(c)

    @b.vector
    def _(v):
        v.wait_ge(c.TT.h, sn2["tt"])
        ins = v.tensor_scalar(c.CR[0:1, 0:1], c.PSA[0:1, 0:1], 1.0 / (nrow * ncol), EPS_Q,
                              ALU.mult, ALU.add)
        n = c.VV.inc(ins)
        v.wait_ge(c.VV.h, n)
        ins = v.reciprocal(c.CR[0:1, 1:2], c.CR[0:1, 0:1])
        c.VV.inc(ins)

    sn3 = snap(c)

    @b.gpsimd
    def _(g):
        g.wait_ge(c.VV.h, sn3["vv"])
        GS = c.sems["gs"]
        ins = g.dma_start(out=c.IGB[0:1, :], in_=c.CR[0:1, 1:2])
        prev = GS.inc(ins, 16)
        k = 1
        while k < 128:
            g.wait_ge(GS.h, prev)
            ins = g.dma_start(out=c.IGB[k:2 * k, :], in_=c.IGB[0:k, :])
            prev = GS.inc(ins, 16)
            k *= 2

    sn4 = snap(c)
    ld2, scl, sgn, qdn, wrt = {}, {}, {}, {}, {}

    order_v = []
    for i in range(nchk):
        order_v.append(("s", i))
        if i >= 1:
            order_v.append(("c", i - 1))
    order_v.append(("c", nchk - 1))
    scl_expect, qdn_expect = {}, {}
    vq = sn4["vv"]
    pos = 0
    for kind, i in order_v:
        pos += 1 if kind == "s" else 2
        if kind == "s":
            scl_expect[i] = vq + pos
        else:
            qdn_expect[i] = vq + pos
    sgn_expect = {i: sn4["aa"] + 2 * (i + 1) for i in range(nchk)}
    GP = [c.sems["g0"], c.sems["g1"]]
    wrt_expect = {i: sn4["g" + str(i % 2)] + 16 * (i // 2 + 1) for i in range(nchk)}

    @b.sync
    def _(s):
        s.wait_ge(c.sems["gs"].h, sn4["gs"])
        for i in range(nchk):
            if i >= 2:
                s.wait_ge(c.VV.h, scl_expect[i - 2])
            ins = s.dma_start(out=wbuf[i % 2], in_=src[i * 128:(i + 1) * 128, :])
            ld2[i] = SSP[i % 2].inc(ins, 16)

    @b.vector
    def _(v):
        for kind, i in order_v:
            if kind == "s":
                if i == 0:
                    v.wait_ge(c.sems["gs"].h, sn4["gs"])
                v.wait_ge(SSP[i % 2].h, ld2[i])
                if i >= 2:
                    v.wait_ge(c.AA.h, sgn_expect[i - 2])
                ins = v.tensor_scalar_mul(rbuf[i % 2], wbuf[i % 2], c.IGB[:])
                scl[i] = c.VV.inc(ins)
                assert scl[i] == scl_expect[i]
            else:
                v.wait_ge(c.AA.h, sgn_expect[i])
                if i >= 2:
                    v.wait_ge(GP[i % 2].h, wrt_expect[i - 2])
                v.tensor_add(qb[i % 2], s1, s2)
                ins = v.tensor_scalar_mul(qb[i % 2], qb[i % 2], 0.5)
                qdn[i] = c.VV.inc(ins, 2)
                assert qdn[i] == qdn_expect[i]

    @b.scalar
    def _(a):
        for i in range(nchk):
            a.wait_ge(c.VV.h, scl_expect[i])
            if i >= 1:
                a.wait_ge(c.VV.h, qdn_expect[i - 1])
            a.activation(s1, rbuf[i % 2], AF.Sign, bias=c.CST[:, 0:1])
            ins = a.activation(s2, rbuf[i % 2], AF.Sign, bias=c.CST[:, 1:2])
            sgn[i] = c.AA.inc(ins, 2)
            assert sgn[i] == sgn_expect[i]

    @b.gpsimd
    def _(g):
        for i in range(nchk):
            g.wait_ge(c.VV.h, qdn_expect[i])
            ins = wrchunk(g, i, qb[i % 2])
            wrt[i] = GP[i % 2].inc(ins, 16)
            assert wrt[i] == wrt_expect[i]


def quant_all(c):
    def wr12(dst):
        def f(g, i, buf):
            return g.dma_start(out=dst[:, :, i, :].rearrange("ht p m -> p ht m"),
                               in_=buf.rearrange("p (ht m) -> p ht m", ht=NHT))
        return f

    def wr3(dst):
        def f(g, i, buf):
            return g.dma_start(out=dst[:, :, i, :].rearrange("dmt p m -> p dmt m"),
                               in_=buf.rearrange("p (dmt m) -> p dmt m", dmt=16))
        return f

    ph_quant(c, c.w1T, DM, FH, wr12(c.w1q))
    ph_quant(c, c.w2T, DM, FH, wr12(c.w2q))
    ph_quant(c, c.w3T, FH, DM, wr3(c.w3q))


def ph_ln(c, src):
    b = c.block
    nh = NCH * 2
    xbuf = [c.XA[:], c.XB[:], c.H2f[:, 0:1024], c.H2f[:, 1024:2048]]
    SSP = [c.sems["s0"], c.sems["s1"], c.sems["x0"], c.sems["x1"]]

    sn0 = snap(c)
    ld, sqd, mmd = {}, {}, {}
    sqd_expect = {i: sn0["vv"] + 2 * (i + 1) for i in range(nh)}
    mmd_expect = {i: sn0["tt"] + (i + 1) for i in range(nh)}

    @b.sync
    def _(s):
        wsnap(c, s, sn0)
        for i in range(nh):
            if i >= 4:
                s.wait_ge(c.VV.h, sqd_expect[i - 4])
            ch, hf = divmod(i, 2)
            ins = s.dma_start(out=xbuf[i % 4],
                              in_=src[ch * 128:(ch + 1) * 128, hf * 1024:(hf + 1) * 1024])
            ld[i] = SSP[i % 4].inc(ins, 16)

    @b.vector
    def _(v):
        for i in range(nh):
            v.wait_ge(SSP[i % 4].h, ld[i])
            if i >= 2:
                v.wait_ge(c.TT.h, mmd_expect[i - 2])
            p = i % 2
            v.tensor_copy(c.A2[:, i * 1024:(i + 1) * 1024], xbuf[i % 4])
            ins = v.tensor_mul(c.SQ[:, p * 1024:(p + 1) * 1024],
                               c.A2[:, i * 1024:(i + 1) * 1024],
                               c.A2[:, i * 1024:(i + 1) * 1024])
            sqd[i] = c.VV.inc(ins, 2)
            assert sqd[i] == sqd_expect[i]

    @b.tensor
    def _(t):
        for i in range(nh):
            ch, hf = divmod(i, 2)
            p = i % 2
            t.wait_ge(c.VV.h, sqd_expect[i])
            for g in range(2):
                cols = slice(hf * 1024 + g * 512, hf * 1024 + (g + 1) * 512)
                bcols = slice(p * 1024 + g * 512, p * 1024 + (g + 1) * 512)
                acols = slice(i * 1024 + g * 512, i * 1024 + (g + 1) * 512)
                t.matmul(c.PSA[0:1, cols], c.OC[:], c.A2[:, acols],
                         start=(ch == 0), stop=(ch == NCH - 1))
                ins = t.matmul(c.PSB[0:1, cols], c.OC[:], c.SQ[:, bcols],
                               start=(ch == 0), stop=(ch == NCH - 1))
            mmd[i] = c.TT.inc(ins)
            assert mmd[i] == mmd_expect[i]

    sn1 = snap(c)

    @b.vector
    def _(v):
        v.wait_ge(c.TT.h, sn1["tt"])
        v.tensor_scalar_mul(c.ACC[0:1, :], c.PSA[0:1, :], 1.0 / DM)
        ins = v.tensor_copy(c.SW[0:1, 0:2048], c.ACC[0:1, :])
        c.VV.inc(ins, 2)

    vr = {}
    for hf in range(2):
        sn2 = snap(c)

        @b.vector
        def _(v):
            cols = slice(hf * 1024, (hf + 1) * 1024)
            if hf == 0:
                v.wait_ge(c.VV.h, sn2["vv"])
            if hf == 1:
                v.wait_ge(c.AA.h, sn2["aa"])
            v.tensor_scalar_mul(c.XA[0:1, :], c.PSB[0:1, cols], 1.0 / DM)
            v.tensor_mul(c.XB[0:1, :], c.ACC[0:1, cols], c.ACC[0:1, cols])
            ins = v.tensor_sub(c.XA[0:1, :], c.XA[0:1, :], c.XB[0:1, :])
            vr[hf] = c.VV.inc(ins)

        sn3 = snap(c)

        @b.scalar
        def _(a):
            a.wait_ge(c.VV.h, vr[hf])
            cols = slice(hf * 1024, (hf + 1) * 1024)
            a.activation(c.XB[0:1, :], c.XA[0:1, :], AF.Ln, bias=c.CST[0:1, 2:3])
            ins = a.activation(c.SQ[0:1, cols], c.XB[0:1, :], AF.Exp, scale=-0.5)
            c.AA.inc(ins, 2)

    sn4 = snap(c)

    @b.tensor
    def _(t):
        t.wait_ge(c.AA.h, sn4["aa"])
        t.wait_ge(c.VV.h, sn4["vv"])
        for g in range(4):
            cols = slice(g * 512, (g + 1) * 512)
            t.matmul(c.PSA[:, cols], c.ORW[:], c.SW[0:1, cols], start=True, stop=True)
            ins = t.matmul(c.PSB[:, cols], c.ORW[:], c.SQ[0:1, cols], start=True, stop=True)
        c.TT.inc(ins)

    sn5 = snap(c)
    nmd = {}
    nmd_expect = {i: sn5["vv"] + 2 * (i + 1) for i in range(nh)}

    @b.vector
    def _(v):
        v.wait_ge(c.TT.h, sn5["tt"])
        for i in range(nh):
            ch, hf = divmod(i, 2)
            pcols = slice(hf * 1024, (hf + 1) * 1024)
            v.tensor_sub(c.ACC[:, (i % 2) * 1024:(i % 2 + 1) * 1024],
                         c.A2[:, i * 1024:(i + 1) * 1024], c.PSA[:, pcols])
            ins = v.tensor_mul(c.A1[:, ch * 2048 + hf * 1024: ch * 2048 + (hf + 1) * 1024],
                               c.ACC[:, (i % 2) * 1024:(i % 2 + 1) * 1024], c.PSB[:, pcols])
            nmd[i] = c.VV.inc(ins, 2)
            assert nmd[i] == nmd_expect[i]


def ph_qkv(c, which):
    b = c.block
    isA = which == "a"
    nq, nk = 16, 4
    ntile = nq + nk
    GS = c.sems["gs"]
    GP = [c.sems["g0"], c.sems["g1"]]
    SSP = [c.sems["s0"], c.sems["s1"]]
    SKD = c.sems["skd"]
    raw = [c.PT0, c.H2[:, 0:2048]]
    qhat = c.H2[:, 2048:4096]
    kout = c.H2[:, 4096:6144]

    sn0 = snap(c)

    @b.gpsimd
    def _(g):
        wsnap(c, g, sn0)
        cs = c.cosA if isA else c.cosB
        sn_ = c.sinA if isA else c.sinBs
        g.dma_start(out=c.TCOS, in_=cs).then_inc(GS.h, 16)
        GS.n += 16
        ins = g.dma_start(out=c.TSIN, in_=sn_)
        GS.inc(ins, 16)

    if isA:
        sn1 = snap(c)

        @b.vector
        def _(v):
            v.wait_ge(GS.h, sn1["gs"])
            ins = v.tensor_scalar_mul(c.TSIN[0:64, :], c.TSIN[0:64, :], -1.0)
            c.VV.inc(ins)

    sn2 = snap(c)

    @b.gpsimd
    def _(g):
        wsnap(c, g, sn2)
        wvsrc = c.wv if isA else c.wvB
        if not isA:
            ins = g.dma_start(out=c.BVR, in_=c.bvr)
            GP[1].inc(ins, 16)
        ins = g.dma_start(out=c.SH[:, 0:8192].rearrange("p (ch m) -> p ch m", ch=NCH),
                          in_=wvsrc.rearrange("(ch p) m -> p ch m", p=128))
        GP[0].inc(ins, 16)

    sn3 = snap(c)
    vmm, vev, vst = {}, {}, {}
    for tt in range(NTT):
        @b.tensor
        def _(t):
            if tt == 0:
                wsnap(c, t, sn3)
            if tt >= 4:
                t.wait_ge(c.AA.h, vev[tt - 4])
            reg = c.PSB[:, (tt % 4) * 512:(tt % 4 + 1) * 512]
            for chk in range(NCH):
                lhs = c.A1[:, chk * 2048 + tt * 128: chk * 2048 + (tt + 1) * 128]
                ins = t.matmul(reg, lhs, c.SH[:, chk * 512:(chk + 1) * 512],
                               start=(chk == 0), stop=(chk == NCH - 1 and isA))
            if not isA:
                ins = t.matmul(reg, c.ORW[:], c.BVR, start=False, stop=True)
            vmm[tt] = c.TT.inc(ins)

        @b.scalar
        def _(a):
            a.wait_ge(c.TT.h, vmm[tt])
            if tt >= 2:
                a.wait_ge(SSP[tt % 2].h, vst[tt - 2])
            reg = c.PSB[:, (tt % 4) * 512:(tt % 4 + 1) * 512]
            ins = a.activation(c.SQ[:, (tt % 2) * 512:(tt % 2) * 512 + 512], reg, AF.Identity)
            vev[tt] = c.AA.inc(ins)

        @b.sync
        def _(s):
            s.wait_ge(c.AA.h, vev[tt])
            dst = c.vtdA if isA else c.vtdB
            ins = s.dma_start(out=dst[tt, :, :],
                              in_=c.SQ[:, (tt % 2) * 512:(tt % 2) * 512 + 512])
            vst[tt] = SSP[tt % 2].inc(ins, 16)

    if isA:
        wsrc = [c.wq[:, t * 128:(t + 1) * 128] for t in range(nq)] + \
               [c.wk[:, t * 128:(t + 1) * 128] for t in range(nk)]
    else:
        wsrc = [c.wqB[:, t * 128:(t + 1) * 128] for t in range(nq)] + \
               [c.wkB[:, t * 128:(t + 1) * 128] for t in range(nk)]
    wsrc = [w.rearrange("(ch p) m -> p ch m", p=128) for w in wsrc]

    raw = [c.PT0, c.H2[:, 0:2048], c.H2[:, 2048:4096]]
    nraw = 3 if isA else 2
    qhat = c.H2[:, 4096:6144]
    kout = c.H2[:, 6144:8192]

    sn4 = snap(c)
    M = {}

    def plan():
        ctr = {k: sn4[k] for k in sn4}

        def put(eng, op, k=1):
            ctr[eng] += k
            M[op] = ctr[eng]

        lag = 2 if isA else 0
        for it in range(ntile + lag + 1):
            tI = it if it < ntile else None
            if tI is not None:
                put("g" + str(tI % 2), ("wld", tI), 16)
            if isA:
                if tI is not None:
                    put("tt", ("proj", tI))
                if 0 <= it - 2 < ntile:
                    put("tt", ("fb", it - 2))
                if 0 <= it - 1 < ntile:
                    put("tt", ("ss", it - 1))
                if tI is not None:
                    put("aa", ("evac", tI))
                if 0 <= it - 1 < ntile:
                    put("aa", ("rows", it - 1), 2)
                j = it - 2
                if 0 <= j < ntile:
                    put("vv", ("qhat", j))
                    put("vv", ("m1", j))
                    put("gs", ("swap", j), 32)
                if tI is not None:
                    put("vv", ("sq", tI))
                if 0 <= j < ntile:
                    put("vv", ("m2", j))
                    put("vv", ("add", j))
                    if j >= nq:
                        put("skd", ("kdma", j), 16)
            else:
                if tI is not None:
                    put("tt", ("proj", tI))
                if 0 <= it - 1 < ntile:
                    put("tt", ("cmm", it - 1))
                if tI is not None:
                    put("aa", ("evac", tI))
                    put("gs", ("swap", tI), 64)
                    put("vv", ("m1", tI))
                    put("vv", ("m2", tI))
                    put("vv", ("add", tI))
                    put("vv", ("csq", tI))
                    if tI >= nq:
                        put("skd", ("kdma", tI), 64)
                if 0 <= it - 1 < ntile:
                    put("vv", ("cmax", it - 1))
    plan()

    def mk(op):
        return M[op]

    lag = 2 if isA else 0
    for it in range(ntile + lag + 1):
        tI = it if it < ntile else None

        if tI is not None:
            @b.gpsimd
            def _(g):
                if tI <= 1:
                    wsnap(c, g, sn4)
                if tI >= 2:
                    g.wait_ge(c.TT.h, mk(("proj", tI - 2)))
                ins = g.dma_start(out=c.WS[:, (tI % 2) * 2048:(tI % 2 + 1) * 2048]
                                  .rearrange("p (ch m) -> p ch m", ch=NCH),
                                  in_=wsrc[tI])
                got = GP[tI % 2].inc(ins, 16)
                assert got == mk(("wld", tI)), (got, mk(("wld", tI)))

        @b.tensor
        def _(t):
            if tI is not None:
                t.wait_ge(GP[tI % 2].h, mk(("wld", tI)))
                if tI == 0:
                    wsnap(c, t, sn4)
                if tI >= 1:
                    t.wait_ge(c.AA.h, mk(("evac", tI - 1)))
                for chk in range(NCH):
                    lhs = c.WS[:, (tI % 2) * 2048 + chk * 128:(tI % 2) * 2048 + (chk + 1) * 128]
                    for g in range(4):
                        ins = t.matmul(c.PSA[:, g * 512:(g + 1) * 512], lhs,
                                       c.A1[:, chk * 2048 + g * 512: chk * 2048 + (g + 1) * 512],
                                       start=(chk == 0), stop=(chk == NCH - 1))
                got = c.TT.inc(ins)
                assert got == mk(("proj", tI))
            if isA:
                j = it - 2
                if 0 <= j < ntile:
                    t.wait_ge(c.AA.h, mk(("rows", j)))
                    if j >= 1:
                        t.wait_ge(c.VV.h, mk(("qhat", j - 1)))
                    wrow = c.WRQ if j < nq else c.WRK
                    for g in range(4):
                        ins = t.matmul(c.PSB[:, g * 512:(g + 1) * 512], wrow[:],
                                       c.RB1[0:1, g * 512:(g + 1) * 512], start=True, stop=True)
                    got = c.TT.inc(ins)
                    assert got == mk(("fb", j))
                j = it - 1
                if 0 <= j < ntile:
                    t.wait_ge(c.VV.h, mk(("sq", j)))
                    if j >= 1:
                        t.wait_ge(c.VV.h, mk(("qhat", j - 1)))
                    for g in range(4):
                        ins = t.matmul(c.PSB[0:1, g * 512:(g + 1) * 512], c.OC[:],
                                       c.SQ[:, g * 512:(g + 1) * 512], start=True, stop=True)
                    got = c.TT.inc(ins)
                    assert got == mk(("ss", j))
            else:
                j = it - 1
                if 0 <= j < ntile:
                    t.wait_ge(c.VV.h, mk(("csq", j)))
                    if j >= 1:
                        t.wait_ge(c.VV.h, mk(("cmax", j - 1)))
                    for g in range(4):
                        ins = t.matmul(c.PSB[0:1, g * 512:(g + 1) * 512], c.OC[:],
                                       c.SW[:, g * 512:(g + 1) * 512], start=True, stop=True)
                    got = c.TT.inc(ins)
                    assert got == mk(("cmm", j))

        if tI is not None:
            @b.scalar
            def _(a):
                a.wait_ge(c.TT.h, mk(("proj", tI)))
                if tI >= nraw:
                    jo = tI - nraw
                    if isA:
                        a.wait_ge(c.VV.h, mk(("sq", jo)))
                        a.wait_ge(c.VV.h, mk(("qhat", jo)))
                    else:
                        a.wait_ge(c.VV.h, mk(("m1", jo)))
                        a.wait_ge(GS.h, mk(("swap", jo)))
                if isA:
                    ins = a.activation(raw[tI % nraw][:], c.PSA[:], AF.Identity)
                else:
                    ins = a.activation(raw[tI % nraw][:], c.PSA[:], AF.Identity,
                                       bias=c.BQS[:, tI:tI + 1])
                got = c.AA.inc(ins)
                assert got == mk(("evac", tI))

        if isA:
            j = it - 1
            if 0 <= j < ntile:
                @b.scalar
                def _(a):
                    a.wait_ge(c.TT.h, mk(("ss", j)))
                    a.activation(c.ACC[0:1, :], c.PSB[0:1, :], AF.Ln,
                                 bias=c.CST[0:1, 3:4], scale=1.0 / DA)
                    ins = a.activation(c.RB1[0:1, :], c.ACC[0:1, :], AF.Exp, scale=-0.5)
                    got = c.AA.inc(ins, 2)
                    assert got == mk(("rows", j))

            j = it - 2
            if 0 <= j < ntile:
                @b.vector
                def _(v):
                    v.wait_ge(c.TT.h, mk(("fb", j)))
                    if j >= 1:
                        v.wait_ge(GS.h, mk(("swap", j - 1)))
                    ins = v.tensor_mul(qhat, raw[j % nraw][:], c.PSB[:])
                    got = c.VV.inc(ins)
                    assert got == mk(("qhat", j))
                    ins = v.tensor_mul(c.PT1[:], qhat, c.TCOS)
                    got = c.VV.inc(ins)
                    assert got == mk(("m1", j))

                @b.gpsimd
                def _(g):
                    g.wait_ge(c.VV.h, mk(("m1", j)))
                    if j >= 1:
                        g.wait_ge(c.VV.h, mk(("m2", j - 1)))
                    g.dma_start(out=c.SW[0:64, :], in_=qhat[64:128, :]).then_inc(GS.h, 16)
                    GS.n += 16
                    ins = g.dma_start(out=c.SW[64:128, :], in_=qhat[0:64, :])
                    got = GS.inc(ins, 16)
                    assert got == mk(("swap", j))

            if tI is not None:
                @b.vector
                def _(v):
                    v.wait_ge(c.AA.h, mk(("evac", tI)))
                    if tI >= 1:
                        v.wait_ge(c.TT.h, mk(("ss", tI - 1)))
                    ins = v.tensor_mul(c.SQ[:], raw[tI % nraw][:], raw[tI % nraw][:])
                    got = c.VV.inc(ins)
                    assert got == mk(("sq", tI))

            if 0 <= j < ntile:
                @b.vector
                def _(v):
                    v.wait_ge(GS.h, mk(("swap", j)))
                    ins = v.tensor_mul(c.SW[:], c.SW[:], c.TSIN)
                    got = c.VV.inc(ins)
                    assert got == mk(("m2", j))
                    isq = j < nq
                    dst = c.A2[:, j * 2048:(j + 1) * 2048] if isq else kout
                    if (not isq) and j - 1 >= nq:
                        v.wait_ge(SKD.h, mk(("kdma", j - 1)))
                    ins = v.tensor_add(dst, c.PT1[:], c.SW[:])
                    got = c.VV.inc(ins)
                    assert got == mk(("add", j))

                if j >= nq:
                    @b.sync
                    def _(s):
                        s.wait_ge(c.VV.h, mk(("add", j)))
                        kv0 = j - nq
                        ins = s.dma_start(out=c.ktdA[kv0, :, :], in_=kout)
                        got = SKD.inc(ins, 16)
                        assert got == mk(("kdma", j))
        else:
            if tI is not None:
                @b.gpsimd
                def _(g):
                    g.wait_ge(c.AA.h, mk(("evac", tI)))
                    if tI >= 1:
                        g.wait_ge(c.TT.h, mk(("cmm", tI - 1)))
                    r = raw[tI % nraw]
                    g.dma_start(out=c.SW[0:32, :], in_=r[32:64, :]).then_inc(GS.h, 16)
                    g.dma_start(out=c.SW[32:64, :], in_=r[0:32, :]).then_inc(GS.h, 16)
                    g.dma_start(out=c.SW[64:96, :], in_=r[96:128, :]).then_inc(GS.h, 16)
                    GS.n += 48
                    ins = g.dma_start(out=c.SW[96:128, :], in_=r[64:96, :])
                    got = GS.inc(ins, 16)
                    assert got == mk(("swap", tI))

                @b.vector
                def _(v):
                    v.wait_ge(c.AA.h, mk(("evac", tI)))
                    ins = v.tensor_mul(c.PT1[:], raw[tI % nraw][:], c.TCOS)
                    got = c.VV.inc(ins)
                    assert got == mk(("m1", tI))
                    v.wait_ge(GS.h, mk(("swap", tI)))
                    ins = v.tensor_mul(c.SW[:], c.SW[:], c.TSIN)
                    got = c.VV.inc(ins)
                    assert got == mk(("m2", tI))
                    isq = tI < nq
                    dst = c.A2[:, tI * 2048:(tI + 1) * 2048] if isq else kout
                    if (not isq) and tI - 1 >= nq:
                        v.wait_ge(SKD.h, mk(("kdma", tI - 1)))
                    ins = v.tensor_add(dst, c.PT1[:], c.SW[:])
                    got = c.VV.inc(ins)
                    assert got == mk(("add", tI))
                    ins = v.tensor_mul(c.SW[:], dst, dst)
                    got = c.VV.inc(ins)
                    assert got == mk(("csq", tI))

                if tI >= nq:
                    @b.sync
                    def _(s):
                        s.wait_ge(c.VV.h, mk(("add", tI)))
                        kv0 = tI - nq
                        s.dma_start(out=c.ktdB[kv0 * 2, 0:64, :], in_=kout[0:64, :]).then_inc(SKD.h, 16)
                        s.dma_start(out=c.ktdB[kv0 * 2, 64:128, :], in_=kout[0:64, :]).then_inc(SKD.h, 16)
                        s.dma_start(out=c.ktdB[kv0 * 2 + 1, 0:64, :], in_=kout[64:128, :]).then_inc(SKD.h, 16)
                        SKD.n += 48
                        ins = s.dma_start(out=c.ktdB[kv0 * 2 + 1, 64:128, :], in_=kout[64:128, :])
                        got = SKD.inc(ins, 16)
                        assert got == mk(("kdma", tI))

            j = it - 1
            if 0 <= j < ntile:
                @b.vector
                def _(v):
                    v.wait_ge(c.TT.h, mk(("cmm", j)))
                    col = 8 + j
                    ins = v.reduce_max(c.CR[0:1, col:col + 1], c.PSB[0:1, :], axis=AX.X)
                    got = c.VV.inc(ins)
                    assert got == mk(("cmax", j))

    sn5 = snap(c)

    @b.vector
    def _(v):
        wsnap(c, v, sn5)
        if isA:
            n = c.VV.inc(v.tensor_reduce(c.CR[0:1, 0:1], c.NRQ[:], axis=AX.X,
                                         op=ALU.max, apply_absolute_value=True))
            n = c.VV.inc(v.tensor_reduce(c.CR[0:1, 1:2], c.NRK[:], axis=AX.X,
                                         op=ALU.max, apply_absolute_value=True))
            v.wait_ge(c.VV.h, n)
            n = c.VV.inc(v.tensor_mul(c.CR[0:1, 2:3], c.CR[0:1, 0:1], c.CR[0:1, 1:2]))
            v.wait_ge(c.VV.h, n)
            ins = v.tensor_scalar_mul(c.CR[0:1, 3:4], c.CR[0:1, 2:3], -math.sqrt(128.0))
            c.VV.inc(ins)
        else:
            n = c.VV.inc(v.tensor_reduce(c.CR[0:1, 0:1], c.CR[0:1, 8:24], axis=AX.X,
                                         op=ALU.max))
            n = c.VV.inc(v.tensor_reduce(c.CR[0:1, 1:2], c.CR[0:1, 24:28], axis=AX.X,
                                         op=ALU.max))
            v.wait_ge(c.VV.h, n)
            ins = v.tensor_mul(c.CR[0:1, 2:3], c.CR[0:1, 0:1], c.CR[0:1, 1:2])
            c.VV.inc(ins)

    if not isA:
        sn6 = snap(c)

        @b.scalar
        def _(a):
            a.wait_ge(c.VV.h, sn6["vv"])
            ins = a.activation(c.CR[0:1, 4:5], c.CR[0:1, 2:3], AF.Sqrt)
            c.AA.inc(ins)

        sn7 = snap(c)

        @b.vector
        def _(v):
            v.wait_ge(c.AA.h, sn7["aa"])
            ins = v.tensor_scalar_mul(c.CR[0:1, 3:4], c.CR[0:1, 4:5], -1.0)
            c.VV.inc(ins)

    sn8 = snap(c)

    @b.vector
    def _(v):
        v.wait_ge(c.VV.h, sn8["vv"])
        ins = v.tensor_copy(c.RB1[0:1, 0:1], c.CR[0:1, 3:4])
        c.VV.inc(ins)

    sn9 = snap(c)

    @b.tensor
    def _(t):
        t.wait_ge(c.VV.h, sn9["vv"])
        ins = t.matmul(c.PSB[:, 0:1], c.ORW[:], c.RB1[0:1, 0:1], start=True, stop=True)
        c.TT.inc(ins)

    sn10 = snap(c)

    @b.vector
    def _(v):
        v.wait_ge(c.TT.h, sn10["tt"])
        v.tensor_copy(c.NEGC[:], c.PSB[:, 0:1])
        ins = v.tensor_copy(c.CR[0:1, 5:6], c.PSB[0:1, 0:1])
        c.VV.inc(ins, 2)

    if isA:
        sn11 = snap(c)

        @b.scalar
        def _(a):
            a.wait_ge(c.VV.h, sn11["vv"])
            ins = a.activation(c.CR[0:1, 6:7], c.CR[0:1, 5:6], AF.Exp)
            c.AA.inc(ins)

        sn12 = snap(c)

        @b.vector
        def _(v):
            v.wait_ge(c.AA.h, sn12["aa"])
            n = c.VV.inc(v.tensor_scalar_mul(c.CR[0:1, 7:8], c.CR[0:1, 6:7], NSINK / 128.0))
            v.wait_ge(c.VV.h, n)
            ins = v.tensor_copy(c.RB1[0:1, 1:2], c.CR[0:1, 7:8])
            c.VV.inc(ins)

        sn13 = snap(c)

        @b.tensor
        def _(t):
            t.wait_ge(c.VV.h, sn13["vv"])
            ins = t.matmul(c.PSB[:, 512:513], c.ORW[:], c.RB1[0:1, 1:2], start=True, stop=True)
            c.TT.inc(ins)

        sn14 = snap(c)

        @b.vector
        def _(v):
            v.wait_ge(c.TT.h, sn14["tt"])
            ins = v.tensor_copy(c.SNK[:, 32:33], c.PSB[:, 512:513])
            c.VV.inc(ins)
    else:
        sn11 = snap(c)

        @b.vector
        def _(v):
            v.wait_ge(c.VV.h, sn11["vv"])
            ins = v.tensor_single_scalar(c.SNKR[0:1, 0:32], c.SNKR[0:1, 0:32],
                                         c.CR[0:1, 5:6], ALU.add)
            c.VV.inc(ins)

        sn12 = snap(c)

        @b.scalar
        def _(a):
            a.wait_ge(c.VV.h, sn12["vv"])
            ins = a.activation(c.SNKR[0:1, 0:32], c.SNKR[0:1, 0:32], AF.Exp)
            c.AA.inc(ins)

        sn13 = snap(c)

        @b.vector
        def _(v):
            v.wait_ge(c.AA.h, sn13["aa"])
            n = c.VV.inc(v.tensor_scalar_mul(c.SNKR[0:1, 0:32], c.SNKR[0:1, 0:32],
                                             1.0 / 128.0))
            v.wait_ge(c.VV.h, n)
            ins = v.tensor_copy(c.RB1[0:1, 2:34], c.SNKR[0:1, 0:32])
            c.VV.inc(ins)

        sn14 = snap(c)

        @b.tensor
        def _(t):
            t.wait_ge(c.VV.h, sn14["vv"])
            for h in range(32):
                ins = t.matmul(c.PSB[:, 512 + h:513 + h], c.ORW[:],
                               c.RB1[0:1, 2 + h:3 + h], start=True, stop=True)
            c.TT.inc(ins)

        sn15 = snap(c)

        @b.vector
        def _(v):
            v.wait_ge(c.TT.h, sn15["tt"])
            ins = v.tensor_copy(c.SNK[:, 0:32], c.PSB[:, 512:544])
            c.VV.inc(ins)


def ph_att_a(c):
    b = c.block
    sn0 = snap(c)
    PT = [c.PT0[:, 0:1024], c.PT0[:, 1024:2048], c.PT1[:, 0:1024], c.PT1[:, 1024:2048]]
    KSL = [c.H2[:, 4096 + i * 128:4096 + (i + 1) * 128] for i in range(4)]
    VSL = [c.H2[:, 4608 + i * 128:4608 + (i + 1) * 128] for i in range(4)]
    SSP = [c.sems["s0"], c.sems["s1"], c.sems["x0"], c.sems["x1"]]
    GR = c.sems["gr"]

    M = {}

    def plan():
        ctr = {k: sn0[k] for k in sn0}

        def put(eng, op, k=1):
            ctr[eng] += k
            M[op] = ctr[eng]

        for h in range(HA):
            for kt in range(NTT):
                put("s" + str((h * NTT + kt) % 4) if (h * NTT + kt) % 4 < 2 else
                    "x" + str((h * NTT + kt) % 4 - 2), ("kld", h, kt), 32)
            put("tt", ("st", h, 0))
            for m in range(1, 32):
                put("tt", ("st", h, m))
                put("tt", ("av", h, m - 1))
            put("tt", ("av", h, 31))
            put("tt", ("dnmm", h))
            for m in range(32):
                put("aa", ("exp", h, m))
            put("aa", ("evac", h))
            for m in range(32):
                put("vv", ("acc", h, m))
            put("vv", ("dncast", h))
            put("vv", ("rowcp", h), 2)
            put("gr", ("dndma", h), 32)
    plan()

    def mk(op):
        return M[op]

    @b.sync
    def _(s):
        wsnap(c, s, sn0)
        for h in range(HA):
            kv = h // 4
            for kt in range(NTT):
                j = h * NTT + kt
                if j >= 4:
                    h2, kt2 = divmod(j - 4, NTT)
                    s.wait_ge(c.TT.h, mk(("av", h2, 2 * kt2 + 1)))
                p = j % 4
                s.dma_start(out=KSL[p],
                            in_=c.ktdA[kv, :, kt * 128:(kt + 1) * 128]).then_inc(SSP[p].h, 16)
                SSP[p].n += 16
                ins = s.dma_start(out=VSL[p],
                                  in_=c.vtdA[kt, :, kv * 128:(kv + 1) * 128])
                got = SSP[p].inc(ins, 16)
                assert got == mk(("kld", h, kt))

    @b.tensor
    def _(t):
        if True:
            wsnap(c, t, sn0)
        for h in range(HA):
            qT = c.A2[:, h * 2048:(h + 1) * 2048]

            def emit_st(m):
                kt, qh = divmod(m, 2)
                gm = h * 32 + m
                t.wait_ge(SSP[(h * NTT + kt) % 4].h, mk(("kld", h, kt)))
                if gm >= 2:
                    h2, m2 = divmod(gm - 2, 32)
                    t.wait_ge(c.AA.h, mk(("exp", h2, m2)))
                par = gm % 2
                reg = c.PSB[:, par * 1024:(par + 1) * 1024]
                lhs = KSL[(h * NTT + kt) % 4]
                t.matmul(reg[:, 0:512], lhs, qT[:, qh * 1024:qh * 1024 + 512],
                         start=True, stop=True)
                ins = t.matmul(reg[:, 512:1024], lhs, qT[:, qh * 1024 + 512:qh * 1024 + 1024],
                               start=True, stop=True)
                got = c.TT.inc(ins)
                assert got == mk(("st", h, m))

            def emit_av(m):
                kt, qh = divmod(m, 2)
                gm = h * 32 + m
                slot = gm % 4
                t.wait_ge(c.AA.h, mk(("exp", h, m)))
                if m <= 1 and h >= 1:
                    t.wait_ge(c.VV.h, mk(("rowcp", h - 1)))
                lhs = VSL[(h * NTT + kt) % 4]
                for g in range(2):
                    ins = t.matmul(c.PSA[:, qh * 1024 + g * 512:qh * 1024 + (g + 1) * 512],
                                   lhs, PT[slot][:, g * 512:(g + 1) * 512],
                                   start=(kt == 0), stop=(kt == NTT - 1),
                                   skip_group_check=True)
                got = c.TT.inc(ins)
                assert got == mk(("av", h, m))

            emit_st(0)
            for m in range(1, 32):
                emit_st(m)
                emit_av(m - 1)
            emit_av(31)
            t.wait_ge(c.AA.h, mk(("evac", h)))
            t.wait_ge(c.VV.h, mk(("dncast", h)))
            for g in range(4):
                ins = t.matmul(c.PSA[0:1, g * 512:(g + 1) * 512], c.OC[:],
                               c.SQ[:, g * 512:(g + 1) * 512], start=True, stop=True,
                               skip_group_check=True)
            got = c.TT.inc(ins)
            assert got == mk(("dnmm", h))

    @b.scalar
    def _(a):
        for h in range(HA):
            for m in range(32):
                gm = h * 32 + m
                par = gm % 2
                slot = gm % 4
                a.wait_ge(c.TT.h, mk(("st", h, m)))
                if gm >= 4:
                    h2, m2 = divmod(gm - 4, 32)
                    a.wait_ge(c.VV.h, mk(("acc", h2, m2)))
                ins = a.activation(PT[slot][:],
                                   c.PSB[:, par * 1024:(par + 1) * 1024],
                                   AF.Exp, bias=c.NEGC[:], scale=SCALE_A)
                got = c.AA.inc(ins)
                assert got == mk(("exp", h, m))
            a.wait_ge(c.TT.h, mk(("av", h, 31)))
            ins = a.activation(c.A1[:, h * 2048:(h + 1) * 2048], c.PSA[:], AF.Identity)
            got = c.AA.inc(ins)
            assert got == mk(("evac", h))

    @b.vector
    def _(v):
        v.wait_ge(c.VV.h, sn0["vv"])
        for h in range(HA):
            for m in range(32):
                kt, qh = divmod(m, 2)
                gm = h * 32 + m
                slot = gm % 4
                v.wait_ge(c.AA.h, mk(("exp", h, m)))
                half = c.ACC[:, qh * 1024:(qh + 1) * 1024]
                if kt == 0:
                    ins = v.tensor_scalar(half, PT[slot][:],
                                          c.SNK[:, 32:33], None, ALU.add)
                else:
                    ins = v.tensor_add(half, half, PT[slot][:])
                got = c.VV.inc(ins)
                assert got == mk(("acc", h, m))
            if h >= 1:
                v.wait_ge(c.TT.h, mk(("dnmm", h - 1)))
            ins = v.tensor_copy(c.SQ[:], c.ACC[:])
            got = c.VV.inc(ins)
            assert got == mk(("dncast", h))
            v.wait_ge(c.TT.h, mk(("dnmm", h)))
            if h >= 1:
                v.wait_ge(GR.h, mk(("dndma", h - 1)))
            v.tensor_copy(c.XA[0:1, :], c.PSA[0:1, 0:1024])
            ins = v.tensor_copy(c.XB[0:1, :], c.PSA[0:1, 1024:2048])
            got = c.VV.inc(ins, 2)
            assert got == mk(("rowcp", h))

    @b.gpsimd
    def _(g):
        for h in range(HA):
            g.wait_ge(c.VV.h, mk(("rowcp", h)))
            g.dma_start(out=c.dnx[h:h + 1, 0:1024], in_=c.XA[0:1, :]).then_inc(GR.h, 16)
            GR.n += 16
            ins = g.dma_start(out=c.dnx[h:h + 1, 1024:2048], in_=c.XB[0:1, :])
            got = GR.inc(ins, 16)
            assert got == mk(("dndma", h))

    _divide_pass(c, nrows=16, nslots=16, slot_heads=1)


def _divide_pass(c, nrows, nslots, slot_heads):
    b = c.block
    sn = snap(c)
    a = 128 // nrows
    ncols = 2048 * nrows // 128
    dnx2 = c.dnx2a if nrows == 16 else c.dnx2b
    GS = c.sems["gs"]
    GR = c.sems["gr"]

    @b.gpsimd
    def _(g):
        wsnap(c, g, sn)
        ins = g.dma_start(out=c.XB[:, 0:ncols],
                          in_=c.dnx[0:nrows, :].rearrange("p (a b) -> (p a) b", a=a))
        GS.inc(ins, 16)

    sn2 = snap(c)

    @b.vector
    def _(v):
        v.wait_ge(GS.h, sn2["gs"])
        ins = v.reciprocal(c.XA[:, 0:ncols], c.XB[:, 0:ncols])
        c.VV.inc(ins)

    sn3 = snap(c)

    @b.gpsimd
    def _(g):
        g.wait_ge(c.VV.h, sn3["vv"])
        ins = g.dma_start(out=dnx2[:, 0:ncols], in_=c.XA[:, 0:ncols])
        GS.inc(ins, 16)

    sn4 = snap(c)
    rbm, bcm, dvm = {}, {}, {}
    for sl in range(nslots):
        for sh in range(slot_heads):
            hh = sl * slot_heads + sh

            @b.gpsimd
            def _(g):
                if hh == 0:
                    g.wait_ge(GS.h, sn4["gs"])
                if hh >= 1:
                    g.wait_ge(c.TT.h, bcm[hh - 1])
                ins = g.dma_start(out=c.RB1[0:1, :],
                                  in_=dnx2[a * hh:a * (hh + 1), 0:ncols].rearrange("a b -> (a b)"))
                rbm[hh] = GR.inc(ins, 16)

            @b.tensor
            def _(t):
                t.wait_ge(GR.h, rbm[hh])
                if sl >= 1 and sh == 0:
                    t.wait_ge(c.VV.h, dvm[sl - 1])
                if slot_heads == 1:
                    lhs = c.ORW[:]
                    rows = slice(0, 128)
                    tp = None
                else:
                    lhs = c.ORW[0:1, 0:64]
                    rows = slice(64 * sh, 64 * (sh + 1))
                    tp = (0, 64) if sh == 1 else None
                for g in range(4):
                    ins = t.matmul(c.PSB[rows, g * 512:(g + 1) * 512], lhs,
                                   c.RB1[0:1, g * 512:(g + 1) * 512], start=True, stop=True,
                                   tile_position=tp)
                bcm[hh] = c.TT.inc(ins)

        @b.vector
        def _(v):
            v.wait_ge(c.TT.h, bcm[sl * slot_heads + slot_heads - 1])
            slot = c.A1[:, sl * 2048:(sl + 1) * 2048]
            v.tensor_mul(c.SQ[:], slot, c.PSB[:])
            ins = v.tensor_copy(slot, c.SQ[:])
            dvm[sl] = c.VV.inc(ins, 2)


def ph_att_b(c):
    b = c.block
    sn0 = snap(c)
    PT = [c.PT0, c.PT1]
    KSL = [c.PT1[:, 1024 + i * 128:1024 + (i + 1) * 128] for i in range(4)]
    VSL = [c.PT1[:, 1536 + i * 64:1536 + (i + 1) * 64] for i in range(4)]
    SSP = [c.sems["s0"], c.sems["s1"], c.sems["x0"], c.sems["x1"]]
    GR = c.sems["gr"]
    accA = c.ACC
    accB = c.H2f[:, 2048:4096]

    def qwin(kt):
        qts = [q for q in (kt - 1, kt, kt + 1) if 0 <= q < NTT]
        return qts

    M = {}

    def plan():
        ctr = {k: sn0[k] for k in sn0}

        def put(eng, op, k=1):
            ctr[eng] += k
            M[op] = ctr[eng]

        for p in range(NTT):
            put("vv", ("init", p), 2)
            for kt in range(NTT):
                j = p * NTT + kt
                put(("s0", "s1", "x0", "x1")[j % 4], ("kld", p, kt), 32)
            for hb in range(2):
                put("tt", ("st", p, 0, hb))
            put("tt", ("zero", p))
            for kt in range(1, NTT):
                for hb in range(2):
                    put("tt", ("st", p, kt, hb))
                for hb in range(2):
                    put("tt", ("av", p, kt - 1, hb))
            for hb in range(2):
                put("tt", ("av", p, NTT - 1, hb))
            put("tt", ("dnmm", p, 0))
            put("tt", ("dnmm", p, 1))
            for kt in range(NTT):
                for hb in range(2):
                    put("aa", ("exp", p, kt, hb))
            put("aa", ("evac", p))
            for kt in range(NTT):
                for hb in range(2):
                    nblk = len(qwin(kt))
                    put("vv", ("mask", p, kt, hb), nblk)
                    put("vv", ("acc", p, kt, hb))
            put("vv", ("dncast", p, 0))
            put("vv", ("rowcp", p, 0), 2)
            put("vv", ("dncast", p, 1))
            put("vv", ("rowcp", p, 1), 2)
            put("gr", ("dndma", p, 0), 32)
            put("gr", ("dndma", p, 1), 32)
    plan()

    def mk(op):
        return M[op]

    @b.sync
    def _(s):
        wsnap(c, s, sn0)
        for p in range(NTT):
            kv = p // 2
            for kt in range(NTT):
                j = p * NTT + kt
                if j >= 4:
                    p2, kt2 = divmod(j - 4, NTT)
                    s.wait_ge(c.TT.h, mk(("av", p2, kt2, 1)))
                pr = j % 4
                s.dma_start(out=KSL[pr],
                            in_=c.ktdB[kv, :, kt * 128:(kt + 1) * 128]).then_inc(SSP[pr].h, 16)
                SSP[pr].n += 16
                ins = s.dma_start(out=VSL[pr],
                                  in_=c.vtdB[kt, :, kv * 64:(kv + 1) * 64])
                got = SSP[pr].inc(ins, 16)
                assert got == mk(("kld", p, kt))

    @b.tensor
    def _(t):
        wsnap(c, t, sn0)
        for p in range(NTT):
            banks_started = set()

            def emit_st(kt, hb):
                j = p * NTT + kt
                t.wait_ge(SSP[j % 4].h, mk(("kld", p, kt)))
                if kt >= 2:
                    t.wait_ge(c.AA.h, mk(("exp", p, kt - 2, hb)))
                elif p >= 1:
                    t.wait_ge(c.AA.h, mk(("exp", p - 1, NTT - 2 + kt, hb)))
                qts = qwin(kt)
                w = 128 * len(qts)
                reg = (kt % 2) * 2 + hb
                lhs = KSL[j % 4][64 * hb:64 * (hb + 1), :]
                rhs = c.A2[64 * hb:64 * (hb + 1),
                           p * 2048 + qts[0] * 128: p * 2048 + (qts[-1] + 1) * 128]
                ins = t.matmul(c.PSB[:, reg * 512:reg * 512 + w], lhs, rhs,
                               start=True, stop=True)
                got = c.TT.inc(ins)
                assert got == mk(("st", p, kt, hb))

            def emit_av(kt, hb):
                j = p * NTT + kt
                t.wait_ge(c.VV.h, mk(("mask", p, kt, hb)))
                qts = qwin(kt)
                c0 = qts[0] * 128
                c1 = (qts[-1] + 1) * 128
                lhs = VSL[j % 4]
                slot = PT[kt % 2]
                ins = None
                cc = c0
                while cc < c1:
                    ce = min(c1, (cc // 512 + 1) * 512)
                    bank = cc // 512
                    lastkt = min(NTT - 1, 4 * bank + 4)
                    ins = t.matmul(c.PSA[64 * hb:64 * (hb + 1), cc:ce], lhs,
                                   slot[:, hb * 512 + (cc - c0):hb * 512 + (ce - c0)],
                                   start=False, stop=(kt == lastkt and hb == 1),
                                   skip_group_check=True,
                                   tile_position=(0, 64) if hb == 1 else None)
                    cc = ce
                got = c.TT.inc(ins)
                assert got == mk(("av", p, kt, hb))

            for hb in range(2):
                emit_st(0, hb)
            if p >= 1:
                t.wait_ge(c.VV.h, mk(("rowcp", p - 1, 1)))
            for g in range(4):
                ins = t.matmul(c.PSA[:, g * 512:(g + 1) * 512], c.ZRW[:],
                               c.TCOS[0:1, 0:512], start=True, stop=False,
                               skip_group_check=True)
            got = c.TT.inc(ins)
            assert got == mk(("zero", p))
            for kt in range(1, NTT):
                for hb in range(2):
                    emit_st(kt, hb)
                for hb in range(2):
                    emit_av(kt - 1, hb)
            for hb in range(2):
                emit_av(NTT - 1, hb)
            for hb in range(2):
                t.wait_ge(c.AA.h, mk(("evac", p)))
                t.wait_ge(c.VV.h, mk(("dncast", p, hb)))
                if hb == 1:
                    t.wait_ge(c.VV.h, mk(("rowcp", p, 0)))
                for g in range(4):
                    ins = t.matmul(c.PSA[0:1, g * 512:(g + 1) * 512], c.OC[:],
                                   c.SW[:, g * 512:(g + 1) * 512], start=True, stop=True,
                                   skip_group_check=True)
                got = c.TT.inc(ins)
                assert got == mk(("dnmm", p, hb))

    @b.scalar
    def _(a):
        for p in range(NTT):
            for kt in range(NTT):
                for hb in range(2):
                    j = p * NTT + kt
                    qts = qwin(kt)
                    w = 128 * len(qts)
                    reg = (kt % 2) * 2 + hb
                    a.wait_ge(c.TT.h, mk(("st", p, kt, hb)))
                    if kt >= 2:
                        a.wait_ge(c.VV.h, mk(("mask", p, kt - 2, hb)))
                    elif p >= 1:
                        a.wait_ge(c.VV.h, mk(("mask", p - 1, NTT - 2 + kt, hb)))
                    sqs = c.SQ[:, reg * 512:reg * 512 + w]
                    ins = a.activation(sqs, c.PSB[:, reg * 512:reg * 512 + w],
                                       AF.Exp, bias=c.NEGC[:])
                    got = c.AA.inc(ins)
                    assert got == mk(("exp", p, kt, hb))
            a.wait_ge(c.TT.h, mk(("av", p, NTT - 1, 1)))
            ins = a.activation(c.A1[:, p * 2048:(p + 1) * 2048], c.PSA[:], AF.Identity)
            got = c.AA.inc(ins)
            assert got == mk(("evac", p))

    @b.vector
    def _(v):
        v.wait_ge(c.VV.h, sn0["vv"])
        for p in range(NTT):
            if p >= 1:
                v.wait_ge(c.TT.h, mk(("dnmm", p - 1, 0)))
                v.wait_ge(c.TT.h, mk(("dnmm", p - 1, 1)))
            v.tensor_scalar(accA[:], c.TCOS, 0.0, c.SNK[:, 2 * p:2 * p + 1],
                            ALU.mult, ALU.add)
            ins = v.tensor_scalar(accB[:], c.TCOS, 0.0, c.SNK[:, 2 * p + 1:2 * p + 2],
                                  ALU.mult, ALU.add)
            got = c.VV.inc(ins, 2)
            assert got == mk(("init", p))
            for kt in range(NTT):
                for hb in range(2):
                    qts = qwin(kt)
                    w = 128 * len(qts)
                    reg = (kt % 2) * 2 + hb
                    v.wait_ge(c.AA.h, mk(("exp", p, kt, hb)))
                    if kt >= 2:
                        v.wait_ge(c.TT.h, mk(("av", p, kt - 2, hb)))
                    elif p >= 1:
                        v.wait_ge(c.TT.h, mk(("av", p - 1, NTT - 2 + kt, hb)))
                    slot = PT[kt % 2]
                    sqs = c.SQ[:, reg * 512:reg * 512 + w]
                    ins = None
                    for bi, qt in enumerate(qts):
                        sblk = sqs[:, bi * 128:(bi + 1) * 128]
                        dblk = slot[:, hb * 512 + bi * 128:hb * 512 + (bi + 1) * 128]
                        if qt == kt - 1:
                            ins = v.tensor_mul(dblk, sblk, c.ML[:])
                        elif qt == kt + 1:
                            ins = v.tensor_mul(dblk, sblk, c.MR[:])
                        else:
                            ins = v.tensor_copy(dblk, sblk)
                    got = c.VV.inc(ins, len(qts))
                    assert got == mk(("mask", p, kt, hb))
                    v.wait_ge(c.VV.h, got)
                    acc = accA if hb == 0 else accB
                    awin = acc[:, qts[0] * 128:(qts[-1] + 1) * 128]
                    ins = v.tensor_add(awin, awin,
                                       slot[:, hb * 512:hb * 512 + w])
                    got = c.VV.inc(ins)
                    assert got == mk(("acc", p, kt, hb))
            for hb in range(2):
                acc = accA if hb == 0 else accB
                if hb == 0:
                    v.wait_ge(c.TT.h, mk(("dnmm", p - 1, 1)) if p >= 1 else 0) if False else None
                    if p >= 1:
                        v.wait_ge(c.TT.h, mk(("dnmm", p - 1, 1)))
                else:
                    v.wait_ge(c.TT.h, mk(("dnmm", p, 0)))
                ins = v.tensor_copy(c.SW[:], acc[:, 0:2048] if hb == 0 else acc)
                got = c.VV.inc(ins)
                assert got == mk(("dncast", p, hb))
                v.wait_ge(c.TT.h, mk(("dnmm", p, hb)))
                if p >= 1 or hb >= 1:
                    prevd = ("dndma", p, 0) if hb == 1 else ("dndma", p - 1, 1)
                    v.wait_ge(GR.h, mk(prevd))
                v.tensor_copy(c.XA[0:1, :], c.PSA[0:1, 0:1024])
                ins = v.tensor_copy(c.XB[0:1, :], c.PSA[0:1, 1024:2048])
                got = c.VV.inc(ins, 2)
                assert got == mk(("rowcp", p, hb))

    @b.gpsimd
    def _(g):
        for p in range(NTT):
            for hb in range(2):
                g.wait_ge(c.VV.h, mk(("rowcp", p, hb)))
                hh = 2 * p + hb
                g.dma_start(out=c.dnx[hh:hh + 1, 0:1024], in_=c.XA[0:1, :]).then_inc(GR.h, 16)
                GR.n += 16
                ins = g.dma_start(out=c.dnx[hh:hh + 1, 1024:2048], in_=c.XB[0:1, :])
                got = GR.inc(ins, 16)
                assert got == mk(("dndma", p, hb))

    _divide_pass(c, nrows=32, nslots=16, slot_heads=2)


def ph_wo(c, wsrc, xsrc, dst, with_bias):
    b = c.block
    sn0 = snap(c)
    M = {}
    GP = [c.sems["g0"], c.sems["g1"]]
    XP = [c.sems["x0"], c.sems["x1"]]
    DP = [c.sems["d0"], c.sems["d1"]]

    def plan():
        ctr = {k: sn0[k] for k in sn0}

        def put(eng, op, k=1):
            ctr[eng] += k
            M[op] = ctr[eng]

        for dmt in range(16):
            put("g" + str(dmt % 2), ("wld", dmt), 16)
            put("tt", ("mm", dmt))
            for hf in range(2):
                i = 2 * dmt + hf
                put("x" + str(i % 2), ("xld", dmt, hf), 16)
                put("vv", ("ev", dmt, hf))
                put("d" + str(i % 2), ("xst", dmt, hf), 16)
    plan()

    def mk(op):
        return M[op]

    PS = [c.PSA, c.PSB]
    X = [c.XA[:], c.XB[:]]
    for dmt in range(16):
        @b.gpsimd
        def _(g):
            if dmt == 0:
                wsnap(c, g, sn0)
            if dmt >= 2:
                g.wait_ge(c.TT.h, mk(("mm", dmt - 2)))
            ins = g.dma_start(out=c.WS[:, (dmt % 2) * 2048:(dmt % 2 + 1) * 2048]
                              .rearrange("p (ch m) -> p ch m", ch=NCH),
                              in_=wsrc[:, dmt * 128:(dmt + 1) * 128]
                              .rearrange("(ch p) m -> p ch m", p=128))
            got = GP[dmt % 2].inc(ins, 16)
            assert got == mk(("wld", dmt))

        @b.tensor
        def _(t):
            if dmt == 0:
                wsnap(c, t, sn0)
            t.wait_ge(GP[dmt % 2].h, mk(("wld", dmt)))
            if dmt >= 2:
                t.wait_ge(c.VV.h, mk(("ev", dmt - 2, 1)))
            for hc in range(16):
                lhs = c.WS[:, (dmt % 2) * 2048 + hc * 128:(dmt % 2) * 2048 + (hc + 1) * 128]
                for g in range(4):
                    ins = t.matmul(PS[dmt % 2][:, g * 512:(g + 1) * 512], lhs,
                                   c.A1[:, hc * 2048 + g * 512: hc * 2048 + (g + 1) * 512],
                                   start=(hc == 0), stop=(hc == 15))
            got = c.TT.inc(ins)
            assert got == mk(("mm", dmt))

        for hf in range(2):
            i = 2 * dmt + hf

            @b.sync
            def _(s):
                if i == 0:
                    wsnap(c, s, sn0)
                if i >= 2:
                    s.wait_ge(c.VV.h, mk(("ev", (i - 2) // 2, (i - 2) % 2)))
                ins = s.dma_start(out=X[i % 2],
                                  in_=xsrc[dmt * 128:(dmt + 1) * 128,
                                           hf * 1024:(hf + 1) * 1024])
                got = XP[i % 2].inc(ins, 16)
                assert got == mk(("xld", dmt, hf))

            @b.vector
            def _(v):
                v.wait_ge(c.TT.h, mk(("mm", dmt)))
                v.wait_ge(XP[i % 2].h, mk(("xld", dmt, hf)))
                if i >= 2:
                    v.wait_ge(DP[i % 2].h, mk(("xst", (i - 2) // 2, (i - 2) % 2)))
                po = PS[dmt % 2][:, hf * 1024:(hf + 1) * 1024]
                ah = c.ACC[:, (i % 2) * 1024:(i % 2 + 1) * 1024]
                if with_bias:
                    ins = v.scalar_tensor_tensor(ah, po, c.BQS[:, 20 + dmt:21 + dmt],
                                                 X[i % 2], ALU.add, ALU.add)
                else:
                    ins = v.tensor_add(ah, po, X[i % 2])
                got = c.VV.inc(ins)
                assert got == mk(("ev", dmt, hf))

            @b.scalar
            def _(a):
                a.wait_ge(c.VV.h, mk(("ev", dmt, hf)))
                ins = a.dma_start(out=dst[dmt * 128:(dmt + 1) * 128,
                                          hf * 1024:(hf + 1) * 1024],
                                  in_=c.ACC[:, (i % 2) * 1024:(i % 2 + 1) * 1024])
                got = DP[i % 2].inc(ins, 16)
                assert got == mk(("xst", dmt, hf))


def ph_ffn(c, xsrc, dst):
    b = c.block
    PS = [c.PSA, c.PSB]
    X = [c.XA[:], c.XB[:]]
    SSP = [c.sems["s0"], c.sems["s1"]]
    XP = [c.sems["x0"], c.sems["x1"]]
    DP = [c.sems["d0"], c.sems["d1"]]

    def hid(ht):
        if ht < 32:
            return c.A2[:, ht * 1024:(ht + 1) * 1024]
        return c.H2[:, (ht - 32) * 1024:(ht - 31) * 1024]

    prev = {}
    for blk in range(2):
        sn0 = snap(c)
        M = {}

        def plan():
            ctr = {k: sn0[k] for k in sn0}

            def put(eng, op, k=1):
                ctr[eng] += k
                M[op] = ctr[eng]

            for ht in range(NHT):
                put("s" + str(ht % 2), ("wld", ht), 32)
                put("tt", ("f1", ht))
                put("aa", ("sig", ht))
                put("vv", ("hm", ht))
            for dmt in range(16):
                put("s" + str(dmt % 2), ("w3", dmt), 16)
                put("tt", ("f2", dmt))
                put("x" + str(dmt % 2), ("xld", dmt), 16)
                put("vv", ("ev", dmt))
                put("d" + str(dmt % 2), ("xst", dmt), 16)
        plan()

        def mk(op):
            return M[op]

        for ht in range(NHT):
            @b.sync
            def _(s):
                if ht == 0:
                    wsnap(c, s, sn0)
                if ht < 2 and blk == 1:
                    s.wait_ge(c.TT.h, prev["f2_15"])
                if ht >= 2:
                    s.wait_ge(c.TT.h, mk(("f1", ht - 2)))
                par = ht % 2
                s.dma_start(out=c.SH[:, par * 4096:par * 4096 + 2048],
                            in_=c.w1q[ht].rearrange("p ch m -> p (ch m)")).then_inc(SSP[par].h, 16)
                SSP[par].n += 16
                ins = s.dma_start(out=c.SH[:, par * 4096 + 2048:par * 4096 + 4096],
                                  in_=c.w2q[ht].rearrange("p ch m -> p (ch m)"))
                got = SSP[par].inc(ins, 16)
                assert got == mk(("wld", ht))

            @b.tensor
            def _(t):
                if ht == 0:
                    wsnap(c, t, sn0)
                t.wait_ge(SSP[ht % 2].h, mk(("wld", ht)))
                if ht < 2 and blk == 1:
                    t.wait_ge(c.VV.h, prev["ev_" + str(14 + ht)])
                if ht >= 2:
                    t.wait_ge(c.AA.h, mk(("sig", ht - 2)))
                    t.wait_ge(c.VV.h, mk(("hm", ht - 2)))
                par = ht % 2
                for w_i in range(2):
                    for chk in range(NCH):
                        lhs = c.SH[:, par * 4096 + w_i * 2048 + chk * 128:
                                   par * 4096 + w_i * 2048 + (chk + 1) * 128]
                        for g in range(2):
                            ins = t.matmul(
                                PS[par][:, w_i * 1024 + g * 512:w_i * 1024 + (g + 1) * 512],
                                lhs,
                                c.A1[:, chk * 2048 + blk * 1024 + g * 512:
                                     chk * 2048 + blk * 1024 + (g + 1) * 512],
                                start=(chk == 0), stop=(chk == NCH - 1))
                got = c.TT.inc(ins)
                assert got == mk(("f1", ht))

            @b.scalar
            def _(a):
                a.wait_ge(c.TT.h, mk(("f1", ht)))
                par = ht % 2
                ins = a.activation(c.PT0[:, par * 1024:(par + 1) * 1024],
                                   PS[par][:, 0:1024], AF.Sigmoid)
                got = c.AA.inc(ins)
                assert got == mk(("sig", ht))

            @b.vector
            def _(v):
                v.wait_ge(c.AA.h, mk(("sig", ht)))
                par = ht % 2
                ins = v.tensor_mul(hid(ht), c.PT0[:, par * 1024:(par + 1) * 1024],
                                   PS[par][:, 1024:2048])
                got = c.VV.inc(ins)
                assert got == mk(("hm", ht))

        W3S = [c.SH[:, 6144:6144 + 5632], c.SH[:, 0:5632]]
        for dmt in range(16):
            @b.sync
            def _(s):
                if dmt >= 2:
                    s.wait_ge(c.TT.h, mk(("f2", dmt - 2)))
                if dmt < 2:
                    s.wait_ge(c.TT.h, mk(("f1", NHT - 1)))
                ins = s.dma_start(out=W3S[dmt % 2],
                                  in_=c.w3q[dmt].rearrange("p ht m -> p (ht m)"))
                got = SSP[dmt % 2].inc(ins, 16)
                assert got == mk(("w3", dmt))

            @b.tensor
            def _(t):
                t.wait_ge(SSP[dmt % 2].h, mk(("w3", dmt)))
                if dmt == 0:
                    t.wait_ge(c.VV.h, mk(("hm", NHT - 1)))
                if dmt >= 2:
                    t.wait_ge(c.VV.h, mk(("ev", dmt - 2)))
                for hc in range(NHT):
                    lhs = W3S[dmt % 2][:, hc * 128:(hc + 1) * 128]
                    for g in range(2):
                        ins = t.matmul(PS[dmt % 2][:, g * 512:(g + 1) * 512], lhs,
                                       hid(hc)[:, g * 512:(g + 1) * 512],
                                       start=(hc == 0), stop=(hc == NHT - 1))
                got = c.TT.inc(ins)
                assert got == mk(("f2", dmt))

            @b.sync
            def _(s):
                if dmt >= 2:
                    s.wait_ge(c.VV.h, mk(("ev", dmt - 2)))
                ins = s.dma_start(out=X[dmt % 2],
                                  in_=xsrc[dmt * 128:(dmt + 1) * 128,
                                           blk * 1024:(blk + 1) * 1024])
                got = XP[dmt % 2].inc(ins, 16)
                assert got == mk(("xld", dmt))

            @b.vector
            def _(v):
                v.wait_ge(c.TT.h, mk(("f2", dmt)))
                v.wait_ge(XP[dmt % 2].h, mk(("xld", dmt)))
                if dmt >= 2:
                    v.wait_ge(DP[dmt % 2].h, mk(("xst", dmt - 2)))
                ah = c.ACC[:, (dmt % 2) * 1024:(dmt % 2 + 1) * 1024]
                ins = v.tensor_add(ah, PS[dmt % 2][:, 0:1024], X[dmt % 2])
                got = c.VV.inc(ins)
                assert got == mk(("ev", dmt))

            @b.scalar
            def _(a):
                a.wait_ge(c.VV.h, mk(("ev", dmt)))
                ins = a.dma_start(out=dst[dmt * 128:(dmt + 1) * 128,
                                          blk * 1024:(blk + 1) * 1024],
                                  in_=c.ACC[:, (dmt % 2) * 1024:(dmt % 2 + 1) * 1024])
                got = DP[dmt % 2].inc(ins, 16)
                assert got == mk(("xst", dmt))

        prev = {"f2_15": M[("f2", 15)], "ev_14": M[("ev", 14)], "ev_15": M[("ev", 15)]}


def final_sync(c):
    b = c.block
    sn = snap(c)

    @b.scalar
    def _(a):
        wsnap(c, a, sn)

    @b.sync
    def _(s):
        wsnap(c, s, sn)


def emit_all(c):
    nsc = c.nc.named_scope

    def ph(tag, fn, *a, **k):
        with nsc("PH_" + tag, notify=True):
            fn(*a, **k)
        return dbg_dump(c, tag)

    ph("init", ph_init, c)
    if ph("ln1", ph_quant_ln1, c):
        return
    if ph("qkva", ph_qkv, c, "a"):
        return
    if ph("atta", ph_att_a, c):
        return
    if ph("woa", ph_wo, c, c.wo, c.xT, c.xm, with_bias=False):
        return
    if ph("ln2", ph_ln, c, c.xm):
        return
    if ph("ffn1", ph_ffn, c, c.xm, c.xm):
        return
    if ph("ln3", ph_ln, c, c.xm):
        return
    if ph("qkvb", ph_qkv, c, "b"):
        return
    if ph("attb", ph_att_b, c):
        return
    if ph("wob", ph_wo, c, c.woB, c.xm, c.xm, with_bias=True):
        return
    if ph("ln4", ph_ln, c, c.xm):
        return
    if ph("ffn2", ph_ffn, c, c.xm, c.yT):
        return
    final_sync(c)


def build_program():
    c = C()
    c.nc = bass.Bass("TRN2", target_bir_lowering=False)
    c.st = contextlib.ExitStack()
    with c.st:
        build(c)
        with c.nc.Block() as blk:
            c.block = blk
            emit_all(c)
    return c.nc


def _host_inputs(inputs):
    f32 = np.float32
    x = np.asarray(inputs["L0_attn_a_hidden_states"], f32)
    cos = np.asarray(inputs["L0_attn_a_cos"], f32)
    sin = np.asarray(inputs["L0_attn_a_sin"], f32)
    com = {}
    com["cosA"] = np.ascontiguousarray(cos[0, :S, 0, :].T)
    com["sinA"] = np.ascontiguousarray(sin[0, :S, 0, :].T)
    com["qnw"] = np.asarray(inputs["L0_attn_a_q_norm_w"], f32).reshape(128, 1)
    com["knw"] = np.asarray(inputs["L0_attn_a_k_norm_w"], f32).reshape(128, 1)
    com["wq"] = np.asarray(inputs["L0_attn_a_wq"], f32)
    com["wk"] = np.asarray(inputs["L0_attn_a_wk"], f32)
    com["wv"] = np.asarray(inputs["L0_attn_a_wv"], f32)
    com["wo"] = np.asarray(inputs["L0_attn_a_wo"], f32)

    perm = np.concatenate([np.arange(0, DB, 2), np.arange(1, DB, 2)])
    WqB = np.asarray(inputs["L0_attn_b_Wq"], f32)[:, :, perm].reshape(DM, HB * DB)
    WkB = np.asarray(inputs["L0_attn_b_Wk"], f32)[:, :, perm].reshape(DM, KVB * DB)
    WvB = np.asarray(inputs["L0_attn_b_Wv"], f32).reshape(DM, KVB * DB)
    WoB = np.asarray(inputs["L0_attn_b_Wo"], f32).reshape(HB * DB, DM)
    bqB = np.asarray(inputs["L0_attn_b_bq"], f32)[:, perm].reshape(HB * DB)
    bkB = np.asarray(inputs["L0_attn_b_bk"], f32)[:, perm].reshape(KVB * DB)
    bvB = np.asarray(inputs["L0_attn_b_bv"], f32).reshape(KVB * DB)
    boB = np.asarray(inputs["L0_attn_b_bo"], f32)
    com["wqB"] = np.ascontiguousarray(WqB)
    com["wkB"] = np.ascontiguousarray(WkB)
    com["wvB"] = np.ascontiguousarray(WvB)
    com["woB"] = np.ascontiguousarray(WoB)
    com["bq16"] = np.ascontiguousarray(bqB.reshape(16, 128).T)
    com["bk4"] = np.ascontiguousarray(bkB.reshape(4, 128).T)
    com["bvr"] = bvB.reshape(1, KVB * DB)
    com["bo16"] = np.ascontiguousarray(boB.reshape(16, 128).T)
    com["sinks"] = np.asarray(inputs["L0_attn_b_sinks"], f32).reshape(1, HB)

    half = DB // 2
    inv = 1.0 / WAVE_B ** (np.arange(half, dtype=f32) * 2.0 / DB)
    tpos = np.arange(S, dtype=f32)
    fr = np.outer(tpos, inv).astype(f32)
    cos32 = np.cos(fr).T.astype(f32)
    sin32 = np.sin(fr).T.astype(f32)
    sc = np.float32(DB ** (-0.25))
    com["cosB"] = np.ascontiguousarray(np.tile(cos32, (4, 1)) * sc)
    com["sinBs"] = np.ascontiguousarray(
        np.concatenate([-sin32, sin32, -sin32, sin32], axis=0) * sc)

    idx = np.arange(128)
    com["mskL"] = (idx[:, None] <= idx[None, :]).astype(f32)
    com["mskR"] = (idx[:, None] >= idx[None, :]).astype(f32)

    com["w1T"] = np.ascontiguousarray(np.asarray(inputs["L0_ffn_w1"], f32).T)
    com["w2T"] = np.ascontiguousarray(np.asarray(inputs["L0_ffn_w2"], f32).T)
    com["w3T"] = np.ascontiguousarray(np.asarray(inputs["L0_ffn_w3"], f32).T)

    in_maps = []
    for bi in range(8):
        m = dict(com)
        m["xT"] = np.ascontiguousarray(x[bi].T)
        in_maps.append(m)
    return in_maps


_RES = {}


def kernel(**inputs):
    in_maps = _host_inputs(inputs)
    nc = build_program()
    res = run_bass_kernel_spmd(nc, in_maps, core_ids=list(range(8)))
    _RES["res"] = res
    out = np.stack([np.asarray(r["yT"]).T for r in res.results], axis=0)
    return np.ascontiguousarray(out.astype(np.float32))




def ph_quant_ln1(c):
    b = c.block
    SHf = c.SH[:].bitcast(F32)
    wb = [SHf[:, 0:1408], SHf[:, 1408:2816]]
    rb = [SHf[:, 2816:4224], SHf[:, 4224:5632]]
    s1 = c.PT0[:, 0:1408]
    s2 = c.PT1[:, 0:1408]
    qb = [c.H2[:, 4096:5504], c.H2[:, 5504:6912]]
    c.QROW = c.ACC[0:1, 0:128]
    QV = c.sems["qv"]
    QA = c.sems["qa"]
    QG = [c.sems["qg0"], c.sems["qg1"]]
    QW = [c.sems["qw0"], c.sems["qw1"]]
    GS = c.sems["gs"]

    mats = [
        (c.w1T, 16, 4, 1408, c.w1q, 11),
        (c.w2T, 16, 4, 1408, c.w2q, 11),
        (c.w3T, 44, 2, 1024, c.w3q, 8),
    ]

    st = {"li": 0, "wi": 0, "cons": {}, "wr": {}, "sg": {}, "cb": {}, "sc": {}}

    def emit_load(m, rc, cq, tag):
        src, nrc, ncq, ccols, dst, hpc = mats[m]
        li = st["li"]
        st["li"] += 1
        par = li % 2

        @b.gpsimd
        def _(g):
            if li >= 2:
                g.wait_ge(QV.h, st["cons"][li - 2])
            ins = g.dma_start(out=wb[par][:, 0:ccols],
                              in_=src[rc * 128:(rc + 1) * 128,
                                      cq * ccols:(cq + 1) * ccols])
            QG[par].inc(ins, 16)
        return li, par

    def step_gld(m, i):
        nrc, ncq, ccols = mats[m][1], mats[m][2], mats[m][3]
        rc, cq = divmod(i, ncq)
        st[("gl", m, i)] = emit_load(m, rc, cq, "g")

    def step_gred(m, i):
        li, par = st[("gl", m, i)]

        @b.vector
        def _(v):
            v.wait_ge(QG[par].h, 16 * (li // 2 + 1))
            ins = v.tensor_reduce(c.GACM[:, i:i + 1], wb[par][:, 0:mats[m][3]],
                                  axis=AX.X, op=ALU.add, apply_absolute_value=True)
            st["cons"][li] = QV.inc(ins)

    def step_gfin(m):
        src, nrc, ncq, ccols, dst, hpc = mats[m]
        nch = nrc * ncq
        nelem = float(nrc * 128 * ncq * ccols)
        sn = snap(c)

        @b.vector
        def _(v):
            v.wait_ge(QV.h, sn["qv"])
            ins = v.tensor_reduce(c.GAC[:], c.GACM[:, 0:nch], axis=AX.X, op=ALU.add)
            QV.inc(ins)

        sn2 = snap(c)

        @b.gpsimd
        def _(g):
            g.wait_ge(QV.h, sn2["qv"])
            ins = g.dma_start(out=c.qscr.rearrange("o p -> p o"), in_=c.GAC[:])
            n = GS.inc(ins, 16)
            g.wait_ge(GS.h, n)
            ins = g.dma_start(out=c.QROW, in_=c.qscr)
            GS.inc(ins, 16)

        sn3 = snap(c)

        @b.vector
        def _(v):
            v.wait_ge(GS.h, sn3["gs"])
            n = QV.inc(v.tensor_reduce(c.IGB3[0:1, 3:4], c.QROW, axis=AX.X, op=ALU.add))
            v.wait_ge(QV.h, n)
            n = QV.inc(v.tensor_scalar(c.IGB3[0:1, 3:4], c.IGB3[0:1, 3:4],
                                       1.0 / nelem, EPS_Q, ALU.mult, ALU.add))
            v.wait_ge(QV.h, n)
            ins = v.reciprocal(c.IGB3[0:1, m:m + 1], c.IGB3[0:1, 3:4])
            QV.inc(ins)

        sn4 = snap(c)

        @b.gpsimd
        def _(g):
            g.wait_ge(QV.h, sn4["qv"])
            k = 1
            prev = None
            while k < 128:
                ins = g.dma_start(out=c.IGB3[k:2 * k, m:m + 1], in_=c.IGB3[0:k, m:m + 1])
                prev = GS.inc(ins, 16)
                g.wait_ge(GS.h, prev)
                k *= 2

    def step_qld(m, i):
        nrc, ncq, ccols = mats[m][1], mats[m][2], mats[m][3]
        rc, cq = divmod(i, ncq)
        st[("ql", m, i)] = emit_load(m, rc, cq, "q")

    def step_qsc(m, i):
        li, par = st[("ql", m, i)]
        ccols = mats[m][3]

        @b.vector
        def _(v):
            v.wait_ge(QG[par].h, 16 * (li // 2 + 1))
            if (m, i - 1) in st["sg"]:
                v.wait_ge(QA.h, st["sg"][(m, i - 1)])
            ins = v.tensor_scalar_mul(rb[i % 2][:, 0:ccols], wb[par][:, 0:ccols],
                                      c.IGB3[:, m:m + 1])
            st["cons"][li] = QV.inc(ins)
            st["sc"][(m, i)] = st["cons"][li]

    def step_qsg(m, i):
        ccols = mats[m][3]

        @b.scalar
        def _(a):
            if m == 0 and i == 0:
                a.wait_ge(c.VV.h, 1)
            a.wait_ge(QV.h, st["sc"][(m, i)])
            if (m, i - 1) in st["cb"]:
                a.wait_ge(QV.h, st["cb"][(m, i - 1)])
            a.activation(s1[:, 0:ccols], rb[i % 2][:, 0:ccols], AF.Sign, bias=c.CST[:, 0:1])
            ins = a.activation(s2[:, 0:ccols], rb[i % 2][:, 0:ccols], AF.Sign, bias=c.CST[:, 1:2])
            st["sg"][(m, i)] = QA.inc(ins)

    def step_qcb(m, i):
        ccols = mats[m][3]

        @b.vector
        def _(v):
            v.wait_ge(QA.h, st["sg"][(m, i)])
            if (m, i - 2) in st["wr"]:
                wi2, wpar2 = st["wr"][(m, i - 2)]
                v.wait_ge(QW[wpar2].h, wi2)
            v.tensor_add(qb[i % 2][:, 0:ccols], s1[:, 0:ccols], s2[:, 0:ccols])
            ins = v.tensor_scalar_mul(qb[i % 2][:, 0:ccols], qb[i % 2][:, 0:ccols], 0.5)
            st["cb"][(m, i)] = QV.inc(ins, 2)

    def step_qwr(m, i):
        src, nrc, ncq, ccols, dst, hpc = mats[m]
        rc, cq = divmod(i, ncq)
        wi = st["wi"]
        st["wi"] += 1
        wpar = wi % 2

        @b.gpsimd
        def _(g):
            g.wait_ge(QV.h, st["cb"][(m, i)])
            ins = g.dma_start(
                out=dst[hpc * cq:hpc * (cq + 1), :, rc, :].rearrange("ht p m -> p ht m"),
                in_=qb[i % 2][:, 0:ccols].rearrange("p (ht m) -> p ht m", ht=hpc))
            st["wr"][(m, i)] = (QW[wpar].inc(ins, 16), wpar)

    steps = []
    for m in range(3):
        nch = mats[m][1] * mats[m][2]
        for i in range(nch):
            steps.append(lambda m=m, i=i: step_gld(m, i))
            steps.append(lambda m=m, i=i: step_gred(m, i))
        steps.append(lambda m=m: step_gfin(m))
        for i in range(nch):
            steps.append(lambda m=m, i=i: step_qld(m, i))
            steps.append(lambda m=m, i=i: step_qsc(m, i))
            steps.append(lambda m=m, i=i: step_qsg(m, i))
            steps.append(lambda m=m, i=i: step_qcb(m, i))
            steps.append(lambda m=m, i=i: step_qwr(m, i))

    lnst = ln_stepper(c, c.xT)
    nln = len(lnst)
    k = 0
    per = max(1, len(steps) // (nln + 1))
    for idx, fn in enumerate(steps):
        fn()
        if (idx + 1) % per == 0 and k < nln:
            lnst[k]()
            k += 1
    while k < nln:
        lnst[k]()
        k += 1


def ln_stepper(c, src):
    b = c.block
    nh = NCH * 2
    xbuf = [c.XA[:], c.XB[:], c.H2f[:, 0:1024], c.H2f[:, 1024:2048]]
    SSP = [c.sems["s0"], c.sems["s1"], c.sems["x0"], c.sems["x1"]]
    sn0 = snap(c)
    ld, sqd, mmd = {}, {}, {}

    def tile(i):
        ch, hf = divmod(i, 2)

        @b.sync
        def _(s):
            if i == 0:
                wsnap(c, s, sn0)
            if i >= 4:
                s.wait_ge(c.VV.h, sqd[i - 4])
            ins = s.dma_start(out=xbuf[i % 4],
                              in_=src[ch * 128:(ch + 1) * 128, hf * 1024:(hf + 1) * 1024])
            ld[i] = SSP[i % 4].inc(ins, 16)

        @b.vector
        def _(v):
            v.wait_ge(SSP[i % 4].h, ld[i])
            if i >= 2:
                v.wait_ge(c.TT.h, mmd[i - 2])
            p = i % 2
            v.tensor_copy(c.A2[:, i * 1024:(i + 1) * 1024], xbuf[i % 4])
            ins = v.tensor_mul(c.SQ[:, p * 1024:(p + 1) * 1024],
                               c.A2[:, i * 1024:(i + 1) * 1024],
                               c.A2[:, i * 1024:(i + 1) * 1024])
            sqd[i] = c.VV.inc(ins, 2)

        @b.tensor
        def _(t):
            p = i % 2
            t.wait_ge(c.VV.h, sqd[i])
            for g in range(2):
                cols = slice(hf * 1024 + g * 512, hf * 1024 + (g + 1) * 512)
                bcols = slice(p * 1024 + g * 512, p * 1024 + (g + 1) * 512)
                acols = slice(i * 1024 + g * 512, i * 1024 + (g + 1) * 512)
                t.matmul(c.PSA[0:1, cols], c.OC[:], c.A2[:, acols],
                         start=(ch == 0), stop=(ch == NCH - 1))
                ins = t.matmul(c.PSB[0:1, cols], c.OC[:], c.SQ[:, bcols],
                               start=(ch == 0), stop=(ch == NCH - 1))
            mmd[i] = c.TT.inc(ins)

    def finish():
        sn1 = snap(c)

        @b.vector
        def _(v):
            v.wait_ge(c.TT.h, sn1["tt"])
            v.tensor_scalar_mul(c.ACC[0:1, :], c.PSA[0:1, :], 1.0 / DM)
            ins = v.tensor_copy(c.SW[0:1, 0:2048], c.ACC[0:1, :])
            c.VV.inc(ins, 2)

        vr = {}
        for hf in range(2):
            sn2 = snap(c)

            @b.vector
            def _(v):
                cols = slice(hf * 1024, (hf + 1) * 1024)
                if hf == 0:
                    v.wait_ge(c.VV.h, sn2["vv"])
                if hf == 1:
                    v.wait_ge(c.AA.h, sn2["aa"])
                v.tensor_scalar_mul(c.XA[0:1, :], c.PSB[0:1, cols], 1.0 / DM)
                v.tensor_mul(c.XB[0:1, :], c.ACC[0:1, cols], c.ACC[0:1, cols])
                ins = v.tensor_sub(c.XA[0:1, :], c.XA[0:1, :], c.XB[0:1, :])
                vr[hf] = c.VV.inc(ins)

            @b.scalar
            def _(a):
                a.wait_ge(c.VV.h, vr[hf])
                cols = slice(hf * 1024, (hf + 1) * 1024)
                a.activation(c.XB[0:1, :], c.XA[0:1, :], AF.Ln, bias=c.CST[0:1, 2:3])
                ins = a.activation(c.SQ[0:1, cols], c.XB[0:1, :], AF.Exp, scale=-0.5)
                c.AA.inc(ins, 2)

        sn4 = snap(c)

        @b.tensor
        def _(t):
            t.wait_ge(c.AA.h, sn4["aa"])
            t.wait_ge(c.VV.h, sn4["vv"])
            for g in range(4):
                cols = slice(g * 512, (g + 1) * 512)
                t.matmul(c.PSA[:, cols], c.ORW[:], c.SW[0:1, cols], start=True, stop=True)
                ins = t.matmul(c.PSB[:, cols], c.ORW[:], c.SQ[0:1, cols], start=True, stop=True)
            c.TT.inc(ins)

        sn5 = snap(c)

        @b.vector
        def _(v):
            v.wait_ge(c.TT.h, sn5["tt"])
            for i in range(nh):
                ch, hf = divmod(i, 2)
                pcols = slice(hf * 1024, (hf + 1) * 1024)
                v.tensor_sub(c.ACC[:, (i % 2) * 1024:(i % 2 + 1) * 1024],
                             c.A2[:, i * 1024:(i + 1) * 1024], c.PSA[:, pcols])
                ins = v.tensor_mul(c.A1[:, ch * 2048 + hf * 1024: ch * 2048 + (hf + 1) * 1024],
                                   c.ACC[:, (i % 2) * 1024:(i % 2 + 1) * 1024], c.PSB[:, pcols])
                c.VV.inc(ins, 2)

    out = [(lambda i=i: tile(i)) for i in range(nh)]
    out.append(finish)
    return out
